# version 03 (final)
# baseline (speedup 1.0000x reference)
import os
import numpy as np
import ml_dtypes
import concourse.bass as bass
import concourse.mybir as mybir
from concourse.bass_core_rust import IndirectOffsetOnAxis
from concourse.bass_utils import run_bass_kernel_spmd
from contextlib import ExitStack

F32 = mybir.dt.float32
BF16 = mybir.dt.bfloat16
I32 = mybir.dt.int32
AF = mybir.ActivationFunctionType
AX = mybir.AxisListType
ALU = mybir.AluOpType
BFNP = ml_dtypes.bfloat16

CSUB = int(os.environ.get("CSUB", "9"))
D = 1024
S = 8192
T = 2048
FF = 4096
E = 8
CL = 448
SCL = E * CL
AGROWS = SCL + 128
NKT = 64
NQ2, Q2 = 8, 256
NQ1, Q1 = 16, 128
NTT = 16
NG = 16
NGQ = 4
EPS = 1e-5
CCAP = 3072.0
XWT = 4
NWIN = NKT // XWT
SCW = [128, 128, 128, 64]
SCO = [0, 128, 256, 384]

A_KT = 0
A_QT = S
A_K0 = S + T
A_V0 = A_K0 + 32
A_BV = A_V0 + D
A_P0 = A_BV + D
A_S1 = A_P0 + Q2
ACOLS = A_S1 + Q2

R_P0 = 0
R_RST = 256
R_REC = 512
R_M = 768
R_MSQ = 1024
R_VAR = 1280
R_LN = 1536
R_RSD = 1792
R_CO = 2048
RCOLS = 2064


class Plan:
    ENGINES = ("tensor", "vector", "scalar", "gpsimd", "sync")

    def __init__(self):
        self.ops = {e: [] for e in self.ENGINES}
        self.cnt = {s: 0 for s in
                    ("csth", "cstg", "xscr", "neg", "xfg0", "xfg1", "xw0", "xw1", "xfq",
                     "w1a", "w1b", "w2a", "w2b", "agld", "zf", "ysb0", "ysb1",
                     "out0", "out1", "ccd", "ind", "mm", "ex", "ve", "cc", "bar")}

    def add(self, eng, fn, waits=(), inc=None, n=1):
        w = [(s, v) for (s, v) in waits if v is not None and v > 0]
        self.ops[eng].append((fn, w, inc, n))
        if inc is None:
            return None
        self.cnt[inc] += n
        return self.cnt[inc]

    def barrier(self):
        tgt = self.cnt["bar"] + len(self.ENGINES)
        self.cnt["bar"] = tgt
        for e in self.ENGINES:
            self.ops[e].append(("BAR_INC", [], None, 0))
            self.ops[e].append((None, [("bar", tgt)], None, 0))


def _dma(out_ap, in_ap):
    def fn(eng):
        return eng.dma_start(out=out_ap, in_=in_ap)
    return fn


def build_program():
    nc = bass.Bass("TRN2", target_bir_lowering=False, num_devices=8)

    d_x = nc.dram_tensor("x", [S, D], F32, kind="ExternalInput")
    d_xT = nc.dram_tensor("xT", [D, S], F32, kind="ExternalInput")
    d_xqT = nc.dram_tensor("xqT", [D, T], F32, kind="ExternalInput")
    d_wqT = nc.dram_tensor("wqT", [D, 128], F32, kind="ExternalInput")
    d_wkT = nc.dram_tensor("wkT", [D, 128], F32, kind="ExternalInput")
    d_wqtl = nc.dram_tensor("wqtl", [128, 8], F32, kind="ExternalInput")
    d_wktl = nc.dram_tensor("wktl", [128, 8], F32, kind="ExternalInput")
    d_bq = nc.dram_tensor("bq", [128, 1], F32, kind="ExternalInput")
    d_bk = nc.dram_tensor("bk", [128, 1], F32, kind="ExternalInput")
    d_btl = nc.dram_tensor("btl", [1, 2], F32, kind="ExternalInput")
    d_k0c = nc.dram_tensor("k0c", [128, 1], F32, kind="ExternalInput")
    d_k0tl = nc.dram_tensor("k0tl", [1, 1], F32, kind="ExternalInput")
    d_v0r = nc.dram_tensor("v0r", [1, D], F32, kind="ExternalInput")
    d_bvr = nc.dram_tensor("bvr", [1, D], F32, kind="ExternalInput")
    d_wvT = nc.dram_tensor("wvT", [D, D], F32, kind="ExternalInput")
    d_rT = nc.dram_tensor("rT", [D, E], F32, kind="ExternalInput")
    d_w1 = nc.dram_tensor("w1", [E, D, FF], F32, kind="ExternalInput")
    d_b1 = nc.dram_tensor("b1r", [E, 128, 32], F32, kind="ExternalInput")
    d_w2 = nc.dram_tensor("w2", [E, FF, D], F32, kind="ExternalInput")
    d_b2 = nc.dram_tensor("b2", [E, D], F32, kind="ExternalInput")
    d_cm = nc.dram_tensor("cmask", [8, 1], F32, kind="ExternalInput")
    d_out = nc.dram_tensor("out", [T, D], F32, kind="ExternalOutput")

    d_xscr = nc.dram_tensor("xscr", [S, D], BF16, kind="Internal")
    d_agscr = nc.dram_tensor("agscr", [AGROWS, D], BF16, kind="Internal")
    d_yscr = nc.dram_tensor("yscr", [AGROWS, D], BF16, kind="Internal")
    d_ccin = nc.dram_tensor("ccin", [1, E], F32, kind="Internal")
    d_ccout = nc.dram_tensor("ccout", [8, E], F32, kind="Internal")

    i_ident = nc.inline_tensor(np.eye(128, dtype=BFNP), name="c_ident")
    i_identf = nc.inline_tensor(np.eye(128, dtype=np.float32), name="c_identf")
    i_lts = nc.inline_tensor(np.tril(np.ones((128, 128), np.float32), -1).astype(BFNP),
                             name="c_lts")
    i_onescb = nc.inline_tensor(np.ones((128, 1), dtype=BFNP), name="c_onescb")
    i_onesrb = nc.inline_tensor(np.ones((1, 128), dtype=BFNP), name="c_onesrb")
    i_onesrf = nc.inline_tensor(np.ones((1, 128), dtype=np.float32), name="c_onesrf")
    i_onesrow = nc.inline_tensor(np.ones((1, ACOLS), dtype=BFNP), name="c_onesrow")
    i_base = nc.inline_tensor(
        np.broadcast_to((np.arange(E, dtype=np.float32) * CL)[None, :], (128, E)).copy(),
        name="c_base")

    P = Plan()
    C = P.cnt

    L = dict(side="left")
    identsb = nc.alloc_sbuf_tensor("identsb", [128, 128], BF16, **L)
    identf = nc.alloc_sbuf_tensor("identf", [128, 128], F32, **L)
    ltssb = nc.alloc_sbuf_tensor("ltssb", [128, 128], BF16, **L)
    onescb = nc.alloc_sbuf_tensor("onescb", [128, 1], BF16, **L)
    onesrb = nc.alloc_sbuf_tensor("onesrb", [1, 128], BF16, **L)
    onesrf = nc.alloc_sbuf_tensor("onesrf", [1, 128], F32, **L)
    basesb = nc.alloc_sbuf_tensor("basesb", [128, E], F32, **L)
    rTsb = nc.alloc_sbuf_tensor("rTsb", [128, 8 * E], BF16, **L)
    cmsb = nc.alloc_sbuf_tensor("cmsb", [8, 1], F32, **L)
    ccsb = nc.alloc_sbuf_tensor("ccsb", [8, E], F32, **L)
    cofT = nc.alloc_sbuf_tensor("cofT", [128, E], F32, **L)
    b1all = nc.alloc_sbuf_tensor("b1all", [128, 32 * E], F32, **L)
    b2all = nc.alloc_sbuf_tensor("b2all", [1, E * D], BF16, **L)
    chp = nc.alloc_sbuf_tensor("chp", [128, NTT], F32, **L)
    posch = nc.alloc_sbuf_tensor("posch", [128, NTT], F32, **L)
    sloti = nc.alloc_sbuf_tensor("sloti", [128, NTT], I32, **L)
    foh_all = nc.alloc_sbuf_tensor("foh_all", [128, NTT * E], F32, **L)
    fohbf = nc.alloc_sbuf_tensor("fohbf", [128, NTT * E], BF16, **L)
    runoff = nc.alloc_sbuf_tensor("runoff", [1, 2 * E], F32, **L)
    rr = nc.alloc_sbuf_tensor("rr", [128, 64], F32, **L)
    ytok = nc.alloc_sbuf_tensor("ytok", [128, D], BF16, **L)
    ysb = [nc.alloc_sbuf_tensor(f"ysb{i}", [128, 512], BF16, **L) for i in range(2)]
    z2b = nc.alloc_sbuf_tensor("z2b", [128, D], F32, **L)
    sqb = nc.alloc_sbuf_tensor("sqb", [128, D], F32, **L)
    outb = [nc.alloc_sbuf_tensor(f"outb{i}", [128, D], F32, **L) for i in range(2)]
    dumb = nc.alloc_sbuf_tensor("dumb", [1, 64], F32, **L)

    for t, dsrc in ((identsb, i_ident), (identf, i_identf), (ltssb, i_lts),
                    (onescb, i_onescb), (onesrb, i_onesrb), (onesrf, i_onesrf),
                    (basesb, i_base)):
        P.add("sync", _dma(t[:, :], dsrc[:, :]), inc="csth", n=16)
    P.add("sync", _dma(cmsb[:, :], d_cm[:, :]), inc="csth", n=16)

    P.add("gpsimd", _dma(d_xscr.ap(), d_x.ap()), inc="xscr", n=16)
    P.add("gpsimd", _dma(rTsb[:, :].rearrange("p (c e) -> p c e", c=8),
                         d_rT.ap().rearrange("(c p) e -> p c e", p=128)), inc="cstg", n=16)
    P.add("gpsimd", _dma(b1all[:, :].rearrange("p (e c) -> p e c", e=E),
                         d_b1.ap().rearrange("e p c -> p e c")), inc="cstg", n=16)
    for _e in range(E):
        P.add("gpsimd", _dma(b2all[0:1, D * _e:D * (_e + 1)], d_b2[_e:_e + 1, :]),
              inc="cstg", n=16)

    R = dict(side="right")
    A_fm = nc.alloc_sbuf_tensor("A_fm", [128, 8 * T], BF16, **R)

    att = ExitStack()

    def ratt(name, shape, dt):
        return att.enter_context(nc.sbuf_tensor(name, shape, dt, side="right"))

    Kf = ratt("Kf", [128, S], BF16)
    arena = ratt("arena", [2, ACOLS], BF16)
    rowsf = ratt("rowsf", [1, RCOLS], F32)
    Qf = ratt("Qf", [128, T], BF16)
    wvTsb = ratt("wvTsb", [128, 8 * D], BF16)
    wqTsb = ratt("wqTsb", [128, 8 * 128], BF16)
    wkTsb = ratt("wkTsb", [128, 8 * 128], BF16)
    wtl = ratt("wtl", [128, 16], BF16)
    k0hi = ratt("k0hi", [128, 1], BF16)
    bqc = ratt("bqc", [128, 1], F32)
    bkc = ratt("bkc", [128, 1], F32)
    btl2 = ratt("btl2", [1, 2], F32)
    scrU = ratt("scrU", [128, 8192], BF16)
    xw = [ratt(f"xw{i}", [128, XWT * 1024], BF16) for i in range(2)]
    xfq = ratt("xfq", [128, 8 * Q2], BF16)
    Pb = [ratt(f"Pb{i}", [128, Q2], BF16) for i in range(2)]
    Gb = ratt("Gb", [128, 8 * Q2], BF16)
    zbuf = ratt("zbuf", [128, 8 * Q2], F32)
    tb1 = ratt("tb1", [128, Q2], F32)
    recsb = ratt("recsb", [128, Q2], F32)
    rmx = ratt("rmx", [128, 18], F32)
    negst = ratt("negst", [1, T], BF16)
    colbf = ratt("colbf", [128, 1], BF16)
    zerot = ratt("zerot", [128, 1024], BF16)
    xfg = [scrU[:, 0:4096], scrU[:, 4096:8192]]
    zsqc = scrU[:, 0:2048]

    aps = ExitStack()
    ps = [aps.enter_context(nc.psum_tensor(f"ps{i}", [128, 512], F32)) for i in range(8)]

    zv = P.add("vector", lambda eng: eng.memset(zerot[:, :], 0.0), inc="ve")
    for r0 in range(0, AGROWS, 128):
        rows = min(128, AGROWS - r0)
        P.add("scalar", _dma(d_agscr[r0:r0 + rows, :], zerot[0:rows, :]),
              waits=[("ve", zv)], inc="zf", n=16)
    P.add("scalar", _dma(d_yscr[SCL:AGROWS, :], zerot[0:128, :]),
          waits=[("ve", zv)], inc="zf", n=16)
    c_zfill = C["zf"]

    P.add("gpsimd", _dma(wtl[:, 0:8], d_wqtl[:, :]), inc="cstg", n=16)
    P.add("gpsimd", _dma(wtl[:, 8:16], d_wktl[:, :]), inc="cstg", n=16)
    P.add("gpsimd", _dma(k0hi[:, :], d_k0c[:, :]), inc="cstg", n=16)
    P.add("gpsimd", _dma(arena[0:1, A_K0:A_K0 + 1], d_k0tl[:, :]), inc="cstg", n=16)
    P.add("gpsimd", _dma(arena[0:1, A_V0:A_V0 + D], d_v0r[:, :]), inc="cstg", n=16)
    P.add("gpsimd", _dma(arena[0:1, A_BV:A_BV + D], d_bvr[:, :]), inc="cstg", n=16)
    P.add("gpsimd", _dma(bqc[:, :], d_bq[:, :]), inc="cstg", n=16)
    P.add("gpsimd", _dma(bkc[:, :], d_bk[:, :]), inc="cstg", n=16)
    P.add("gpsimd", _dma(btl2[:, :], d_btl[:, :]), inc="cstg", n=16)
    P.add("gpsimd", _dma(wqTsb[:, :].rearrange("p (c f) -> p c f", c=8),
                         d_wqT.ap().rearrange("(c p) f -> p c f", p=128)), inc="cstg", n=16)
    P.add("gpsimd", _dma(wkTsb[:, :].rearrange("p (c f) -> p c f", c=8),
                         d_wkT.ap().rearrange("(c p) f -> p c f", p=128)), inc="cstg", n=16)
    P.add("gpsimd", _dma(wvTsb[:, :].rearrange("p (c f) -> p c f", c=8),
                         d_wvT.ap().rearrange("(c p) f -> p c f", p=128)), inc="cstg", n=16)

    P.add("sync", _dma(arena[1:2, 0:ACOLS], i_onesrow[:, :]), inc="csth", n=16)
    c_cst = (C["csth"], C["cstg"])

    kf_mm, kf_ev = {}, {}
    for g in range(NG + NGQ):
        isq = g >= NG
        gg = g - NG if isq else g
        par = g % 2
        src = d_xqT if isq else d_xT
        ld = P.add("gpsimd", _dma(xfg[par].rearrange("p (c t) -> p c t", c=8),
                                  src[:, 512 * gg:512 * (gg + 1)]
                                  .rearrange("(c p) t -> p c t", p=128)),
                   waits=[("mm", kf_mm.get(g - 2))], inc=f"xfg{par}", n=16)
        wT = wqTsb if isq else wkTsb
        bias_c = bqc if isq else bkc
        bias_t = btl2[0:1, 0:1] if isq else btl2[0:1, 1:2]
        dst = Qf if isq else Kf
        trow = A_QT if isq else A_KT
        bank = ps[5 + par]

        def mk_proj(pp=par, w=wT):
            def fn(eng):
                last = None
                for ci in range(8):
                    eng.matmul(ps[5 + pp][:, 0:512], w[:, 128 * ci:128 * (ci + 1)],
                               xfg[pp][:, 512 * ci:512 * (ci + 1)],
                               start=(ci == 0), stop=(ci == 7))
                for ci in range(8):
                    last = eng.matmul(ps[7][0:1, 0:512],
                                      wtl[:, (0 if w is wqTsb else 8) + ci:
                                          (1 if w is wqTsb else 9) + ci],
                                      xfg[pp][:, 512 * ci:512 * (ci + 1)],
                                      start=(ci == 0), stop=(ci == 7))
                return last
            return fn
        mmw = [(f"xfg{par}", ld), ("csth", c_cst[0]), ("cstg", c_cst[1])]
        if g >= 1:
            mmw.append(("ex", kf_ev[g - 1]))
        kf_mm[g] = P.add("tensor", mk_proj(), waits=mmw, inc="mm")

        def mk_ev(pp=par, ds=dst, bc=bias_c, bt=bias_t, tr=trow, g0=gg):
            def fn(eng):
                eng.activation(ds[:, 512 * g0:512 * (g0 + 1)], ps[5 + pp][:, 0:512],
                               AF.Identity, bias=bc[:, :], scale=1.0)
                return eng.activation(arena[0:1, tr + 512 * g0:tr + 512 * (g0 + 1)],
                                      ps[7][0:1, 0:512], AF.Identity, bias=bt, scale=1.0)
            return fn
        kf_ev[g] = P.add("scalar", mk_ev(), waits=[("mm", kf_mm[g])], inc="ex")
    P.barrier()

    b_ex = {}
    prev_red15 = None
    for qt in range(NQ1):
        q0 = Q1 * qt
        red = {}
        for j in range(16):
            bank = ps[5 + (j % 2)]

            def mk_s1(jj=j, q0=q0, bb=bank):
                def fn(eng):
                    eng.matmul(bb[:, 0:512], Qf[:, q0:q0 + Q1],
                               Kf[:, 512 * jj:512 * (jj + 1)], start=True, stop=False)
                    return eng.matmul(bb[:, 0:512], arena[0:1, A_QT + q0:A_QT + q0 + Q1],
                                      arena[0:1, A_KT + 512 * jj:A_KT + 512 * (jj + 1)],
                                      start=False, stop=True)
                return fn
            w = []
            if j >= 2:
                w.append(("ve", red[j - 2]))
            elif prev_red15 is not None:
                w.append(("ve", prev_red15))
            mmc = P.add("tensor", mk_s1(), waits=w, inc="mm")
            red[j] = P.add("vector",
                           (lambda jj=j, bb=bank: (lambda eng: eng.reduce_max(
                               rmx[:, jj:jj + 1], bb[:, 0:512], AX.X)))(),
                           waits=[("mm", mmc)], inc="ve")
        prev_red15 = red[15]

        def mk_k0(q0=q0):
            def fn(eng):
                eng.matmul(ps[4][:, 0:1], Qf[:, q0:q0 + Q1], k0hi[:, :],
                           start=True, stop=False)
                return eng.matmul(ps[4][:, 0:1], arena[0:1, A_QT + q0:A_QT + q0 + Q1],
                                  arena[0:1, A_K0:A_K0 + 1], start=False, stop=True)
            return fn
        wk0 = [("ex", b_ex[qt - 1])] if qt > 0 else []
        k0m = P.add("tensor", mk_k0(), waits=wk0, inc="mm")
        vk0 = P.add("vector", lambda eng: eng.reduce_max(rmx[:, 16:17], ps[4][:, 0:1], AX.X),
                    waits=[("mm", k0m)], inc="ve")
        vmx = P.add("vector", lambda eng: eng.reduce_max(rmx[:, 17:18], rmx[:, 0:17], AX.X),
                    waits=[("ve", red[15]), ("ve", vk0)], inc="ve")
        sbf = P.add("scalar", lambda eng: eng.copy(colbf[:, :], rmx[:, 17:18]),
                    waits=[("ve", vmx)], inc="ex")
        mw = [("ex", sbf), ("ve", vk0)]
        if qt == 0:
            mw.append(("csth", c_cst[0]))
            mw.append(("cstg", c_cst[1]))
        mrow = P.add("tensor", lambda eng: eng.matmul(ps[4][0:1, 256:384], colbf[:, :],
                                                      identsb[:, :], start=True, stop=True),
                     waits=mw, inc="mm")
        b_ex[qt] = P.add("scalar",
                         (lambda q0=q0: (lambda eng: eng.activation(
                             negst[0:1, q0:q0 + Q1], ps[4][0:1, 256:384],
                             AF.Identity, bias=0.0, scale=-1.0)))(),
                         waits=[("mm", mrow)], inc="ex")
    c_neg_ld = P.add("sync", _dma(arena[1:2, A_QT:A_QT + T], negst[0:1, :]),
                     waits=[("ex", b_ex[NQ1 - 1])], inc="neg", n=16)
    P.barrier()

    gmc_all = {}
    prev_ve = None
    prev_ex = None
    for qt in range(min(NQ2, int(os.environ.get('QLIM', '8')))):
        q0 = Q2 * qt
        xfq_ld = P.add("gpsimd", _dma(xfq[:, :].rearrange("p (c t) -> p c t", c=8),
                                      d_xqT[:, q0:q0 + Q2].rearrange("(c p) t -> p c t", p=128)),
                       waits=[("ve", prev_ve)], inc="xfq", n=16)

        def mk_k0r(q0=q0):
            def fn(eng):
                eng.matmul(ps[7][0:1, 0:Q2], k0hi[:, :], Qf[:, q0:q0 + Q2],
                           start=True, stop=False)
                return eng.matmul(ps[7][0:1, 0:Q2], arena[0:2, A_K0:A_K0 + 1],
                                  arena[0:2, A_QT + q0:A_QT + q0 + Q2],
                                  start=False, stop=True)
            return fn
        wk = [("neg", 16), ("csth", c_cst[0]), ("cstg", c_cst[1])]
        if prev_ve is not None:
            wk += [("ve", prev_ve), ("ex", prev_ex)]
        k0mm = P.add("tensor", mk_k0r(), waits=wk, inc="mm")

        p0ex1 = P.add("scalar",
                      lambda eng: eng.activation(rowsf[0:1, R_P0:R_P0 + Q2],
                                                 ps[7][0:1, 0:Q2], AF.Exp),
                      waits=[("mm", k0mm)], inc="ex")
        p0ex = P.add("scalar",
                     lambda eng: eng.copy(arena[0:1, A_P0:A_P0 + Q2],
                                          rowsf[0:1, R_P0:R_P0 + Q2]),
                     waits=[("ex", p0ex1)], inc="ex")

        exc, gmc, xwld = {}, {}, {}
        for kc in range(NKT + 1):
            if kc < NKT:
                wd = kc // XWT
                par = wd % 2
                if kc % XWT == 0:
                    wws = [("xscr", 16)]
                    key2 = (qt, wd - 2) if wd >= 2 else (qt - 1, wd + NWIN - 2)
                    if key2 in gmc_all:
                        wws.append(("mm", gmc_all[key2]))
                    xwld[wd] = P.add("sync",
                                     _dma(xw[par][:, :].rearrange("p (j f) -> p j f", j=XWT),
                                          d_xscr[XWT * 128 * wd:XWT * 128 * (wd + 1), :]
                                          .rearrange("(j p) f -> p j f", p=128)),
                                     waits=wws, inc=f"xw{par}", n=16)
                sbank = ps[5 + (kc % 2)]

                def mk_sc(kk=kc, q0=q0, bb=sbank):
                    def fn(eng):
                        eng.matmul(bb[:, 0:Q2], Kf[:, 128 * kk:128 * (kk + 1)],
                                   Qf[:, q0:q0 + Q2], start=True, stop=False)
                        return eng.matmul(bb[:, 0:Q2],
                                          arena[0:2, A_KT + 128 * kk:A_KT + 128 * (kk + 1)],
                                          arena[0:2, A_QT + q0:A_QT + q0 + Q2],
                                          start=False, stop=True)
                    return fn
                w = [("ex", exc[kc - 2])] if kc >= 2 else ([("ve", prev_ve)] if prev_ve else [])
                smm = P.add("tensor", mk_sc(), waits=w, inc="mm")
                exc[kc] = P.add("scalar",
                                (lambda kk=kc, bb=sbank: (lambda eng: eng.activation(
                                    Pb[kk % 2][:, :], bb[:, 0:Q2], AF.Exp)))(),
                                waits=[("mm", smm)], inc="ex")
            gk = kc - 1
            if gk >= 0:
                gwd = gk // XWT
                gpar = gwd % 2

                def mk_g(kk=gk, pp=gk % 2, wpar=gpar):
                    def fn(eng):
                        for dxc in range(8):
                            eng.matmul(ps[dxc // 2][:, 256 * (dxc % 2):256 * (dxc % 2) + Q2],
                                       xw[wpar][:, 1024 * (kk % XWT) + 128 * dxc:
                                                1024 * (kk % XWT) + 128 * (dxc + 1)],
                                       Pb[pp][:, :],
                                       start=(kk == 0 and dxc % 2 == 0),
                                       stop=(kk == NKT - 1),
                                       skip_group_check=True)
                        return eng.matmul(ps[4][0:1, 0:Q2], onescb[:, :], Pb[pp][:, :],
                                          start=(kk == 0), stop=(kk == NKT - 1))
                    return fn
                gw = [("ex", exc[gk]), (f"xw{gwd % 2}", xwld[gwd])]
                if gk == 0 and prev_ex is not None:
                    gw += [("ex", prev_ex), ("ve", prev_ve)]
                gmc[gk] = P.add("tensor", mk_g(), waits=gw, inc="mm")
                gmc_all[(qt, gwd)] = gmc[gk]

        if CSUB < 2:
            prev_ve = C["ve"]
            prev_ex = C["ex"]
            continue
        s1ex = P.add("scalar", lambda eng: eng.activation(arena[0:1, A_S1:A_S1 + Q2],
                                                          ps[4][0:1, 0:Q2], AF.Identity,
                                                          bias=0.0, scale=1.0),
                     waits=[("mm", gmc[NKT - 1]), ("ex", p0ex)], inc="ex")
        if int(os.environ.get("BSUB", "9")) < 2:
            prev_ve = C["ve"]; prev_ex = C["ex"]; continue
        vrt = P.add("vector", lambda eng: eng.tensor_tensor(rowsf[0:1, R_RST:R_RST + Q2],
                                                            ps[4][0:1, 0:Q2],
                                                            rowsf[0:1, R_P0:R_P0 + Q2], ALU.add),
                    waits=[("mm", gmc[NKT - 1]), ("ex", s1ex)], inc="ve")
        vrec = P.add("vector", lambda eng: eng.reciprocal(rowsf[0:1, R_REC:R_REC + Q2],
                                                          rowsf[0:1, R_RST:R_RST + Q2]),
                     waits=[("ve", vrt)], inc="ve")
        if int(os.environ.get("BSUB", "9")) < 3:
            prev_ve = C["ve"]; prev_ex = C["ex"]; continue
        rcmm = P.add("tensor", lambda eng: eng.matmul(ps[7][:, 0:Q2], onesrf[:, :],
                                                      rowsf[0:1, R_REC:R_REC + Q2],
                                                      start=True, stop=True),
                     waits=[("ve", vrec), ("ex", s1ex)], inc="mm")
        rcev = P.add("scalar", lambda eng: eng.copy(recsb[:, :], ps[7][:, 0:Q2]),
                     waits=[("mm", rcmm)], inc="ex")

        if int(os.environ.get("BSUB", "9")) < 4:
            prev_ve = C["ve"]; prev_ex = C["ex"]; continue
        gev = {}
        for ci in range(8):
            gev[ci] = P.add("scalar",
                            (lambda cc=ci: (lambda eng: eng.copy(
                                Gb[:, 256 * cc:256 * (cc + 1)],
                                ps[cc // 2][:, 256 * (cc % 2):256 * (cc % 2) + Q2])))(),
                            waits=[("mm", gmc[NKT - 1])], inc="ex")

        if CSUB < 3:
            prev_ve = C["ve"]
            prev_ex = C["ex"]
            continue
        vz = {}
        zex = {}
        s2last = None
        for dvc in range(8):
            bank = ps[5 + (dvc % 2)]

            def mk_s2(dd=dvc, bb=bank):
                def fn(eng):
                    for ci in range(8):
                        eng.matmul(bb[:, 0:Q2],
                                   wvTsb[:, 1024 * ci + 128 * dd:1024 * ci + 128 * (dd + 1)],
                                   Gb[:, 256 * ci:256 * (ci + 1)], start=(ci == 0), stop=False)
                    eng.matmul(bb[:, 0:Q2], arena[0:1, A_BV + 128 * dd:A_BV + 128 * (dd + 1)],
                               arena[0:1, A_S1:A_S1 + Q2], start=False, stop=False)
                    return eng.matmul(bb[:, 0:Q2],
                                      arena[0:1, A_V0 + 128 * dd:A_V0 + 128 * (dd + 1)],
                                      arena[0:1, A_P0:A_P0 + Q2], start=False, stop=True)
                return fn
            w = [("ex", gev[7]), ("ex", exc[NKT - 1]), ("ex", s1ex)]
            if dvc >= 2:
                w.append(("ve", vz[dvc - 2]))
            s2 = P.add("tensor", mk_s2(), waits=w, inc="mm")
            s2last = s2
            P.add("vector",
                  (lambda bb=bank: (lambda eng: eng.tensor_tensor(
                      tb1[:, :], bb[:, 0:Q2], recsb[:, :], ALU.mult)))(),
                  waits=[("mm", s2), ("ex", rcev)], inc="ve")
            vz[dvc] = P.add("vector",
                            (lambda dd=dvc: (lambda eng: eng.scalar_tensor_tensor(
                                zbuf[:, 256 * dd:256 * (dd + 1)],
                                xfq[:, 256 * dd:256 * (dd + 1)], 2.0, tb1[:, :],
                                ALU.mult, ALU.add)))(),
                            waits=[("xfq", xfq_ld)], inc="ve")
        if CSUB < 4:
            prev_ve = C["ve"]
            prev_ex = C["ex"]
            continue
        for dvc in range(8):
            zb_c = P.add("scalar",
                         (lambda dd=dvc: (lambda eng: eng.copy(
                             Gb[:, 256 * dd:256 * (dd + 1)],
                             zbuf[:, 256 * dd:256 * (dd + 1)])))(),
                         waits=[("ve", vz[dvc]), ("mm", s2last)], inc="ex")
            zex[dvc] = P.add("scalar",
                             (lambda dd=dvc: (lambda eng: eng.activation(
                                 zsqc[:, 256 * dd:256 * (dd + 1)],
                                 Gb[:, 256 * dd:256 * (dd + 1)], AF.Square)))(),
                             waits=[("ex", zb_c)], inc="ex")

        def mk_stats(eng):
            last = None
            for dd in range(8):
                eng.matmul(ps[4][0:1, 256:512], onescb[:, :], Gb[:, 256 * dd:256 * (dd + 1)],
                           start=(dd == 0), stop=(dd == 7))
            for dd in range(8):
                last = eng.matmul(ps[4][32:33, 256:512], onescb[:, :],
                                  zsqc[:, 256 * dd:256 * (dd + 1)],
                                  start=(dd == 0), stop=(dd == 7), skip_group_check=True)
            return last
        stmm = P.add("tensor", mk_stats, waits=[("ex", zex[7]), ("ve", vrt)], inc="mm")

        if CSUB < 5:
            prev_ve = C["ve"]
            prev_ex = C["ex"]
            continue
        vm = P.add("vector", lambda eng: eng.tensor_scalar_mul(rowsf[0:1, R_M:R_M + Q2],
                                                               ps[4][0:1, 256:512], 1.0 / D),
                   waits=[("mm", stmm)], inc="ve")
        vq = P.add("vector", lambda eng: eng.tensor_scalar_mul(rowsf[0:1, R_MSQ:R_MSQ + Q2],
                                                               ps[4][32:33, 256:512], 1.0 / D),
                   inc="ve")
        P.add("vector", lambda eng: eng.tensor_tensor(rowsf[0:1, R_VAR:R_VAR + Q2],
                                                      rowsf[0:1, R_M:R_M + Q2],
                                                      rowsf[0:1, R_M:R_M + Q2], ALU.mult),
              inc="ve")
        P.add("vector", lambda eng: eng.tensor_tensor(rowsf[0:1, R_VAR:R_VAR + Q2],
                                                      rowsf[0:1, R_MSQ:R_MSQ + Q2],
                                                      rowsf[0:1, R_VAR:R_VAR + Q2], ALU.subtract),
              inc="ve")
        vvp = P.add("vector", lambda eng: eng.tensor_scalar_add(rowsf[0:1, R_VAR:R_VAR + Q2],
                                                                rowsf[0:1, R_VAR:R_VAR + Q2], EPS),
                    inc="ve")
        sln = P.add("scalar", lambda eng: eng.activation(rowsf[0:1, R_LN:R_LN + Q2],
                                                         rowsf[0:1, R_VAR:R_VAR + Q2], AF.Ln),
                    waits=[("ve", vvp)], inc="ex")
        srs = P.add("scalar", lambda eng: eng.activation(rowsf[0:1, R_RSD:R_RSD + Q2],
                                                         rowsf[0:1, R_LN:R_LN + Q2], AF.Exp,
                                                         scale=-0.5),
                    waits=[("ex", sln)], inc="ex")
        mtm = P.add("tensor", lambda eng: eng.matmul(ps[7][:, 256:512], onesrf[:, :],
                                                     rowsf[0:1, R_M:R_M + Q2], start=False,
                                                     stop=True, skip_group_check=True),
                    waits=[("ve", vm), ("ex", rcev)], inc="mm")
        rsm = P.add("tensor", lambda eng: eng.matmul(ps[4][:, 0:Q2], onesrf[:, :],
                                                     rowsf[0:1, R_RSD:R_RSD + Q2], start=True,
                                                     stop=True, skip_group_check=True),
                    waits=[("ex", srs), ("ve", vq)], inc="mm")
        lastve = None
        for dvc in range(8):
            P.add("vector",
                  (lambda dd=dvc: (lambda eng: eng.tensor_tensor(
                      tb1[:, :], zbuf[:, 256 * dd:256 * (dd + 1)], ps[7][:, 256:512],
                      ALU.subtract)))(),
                  waits=[("mm", rsm)], inc="ve")
            lastve = P.add("vector",
                           (lambda dd=dvc, q0=q0: (lambda eng: eng.tensor_tensor(
                               A_fm[:, 2048 * dd + q0:2048 * dd + q0 + Q2], tb1[:, :],
                               ps[4][:, 0:Q2], ALU.mult)))(), inc="ve")
        prev_ve = lastve
        prev_ex = C["ex"]

    P.barrier()
    aps.close()
    att.close()
    c_attn_ve = C["ve"]

    ffn = ExitStack()

    def rffn(name, shape, dt):
        return ffn.enter_context(nc.sbuf_tensor(name, shape, dt, side="right"))

    A_tok = rffn("A_tok", [128, NTT * 1024], BF16)
    agbuf = [rffn(f"agbuf{i}", [128, 8 * CL], BF16) for i in range(2)]
    agld = rffn("agld", [128, 1024], BF16)
    w1q = [rffn(f"w1q{i}", [128, 8 * 512], BF16) for i in range(2)]
    w2q = [rffn(f"w2q{i}", [128, 8 * 1024], BF16) for i in range(2)]
    hsb = rffn("hsb", [128, 32 * CL], BF16)

    dps = ExitStack()
    pbf = [dps.enter_context(nc.psum_tensor(f"pbf{i}", [128, 128], F32)) for i in range(2)]
    pf = [dps.enter_context(nc.psum_tensor(f"pf{i}", [128, 512], F32)) for i in range(4)]

    v_last = {}
    v_runoff = {}
    v_rz = P.add("vector", lambda eng: eng.memset(runoff[0:1, 0:E], 0.0),
                 waits=[("csth", c_cst[0]), ("cstg", c_cst[1])], inc="ve")
    for tt in range(NTT):
        bank = pf[tt % 2]

        def mk_rt(t0=tt, bb=bank):
            def fn(eng):
                last = None
                for ci in range(8):
                    last = eng.matmul(bb[:, 0:8],
                                      A_fm[:, 2048 * ci + 128 * t0:2048 * ci + 128 * (t0 + 1)],
                                      rTsb[:, 8 * ci:8 * (ci + 1)],
                                      start=(ci == 0), stop=(ci == 7))
                return last
            return fn
        w = [("csth", c_cst[0]), ("cstg", c_cst[1]), ("ve", c_attn_ve)]
        if tt >= 2:
            w.append(("ve", v_last[tt - 2]))
        rmm = P.add("tensor", mk_rt(), waits=w, inc="mm")

        def AV(fn, waits=(), inc="ve"):
            return P.add("vector", fn, waits=waits, inc=inc)

        AV((lambda bb=bank: (lambda eng: eng.reduce_max(rr[:, 0:1], bb[:, 0:8], AX.X)))(),
           waits=[("mm", rmm)])
        vneg = AV(lambda eng: eng.tensor_scalar_mul(rr[:, 1:2], rr[:, 0:1], -1.0))
        sel = P.add("scalar",
                    (lambda bb=bank: (lambda eng: eng.activation(
                        rr[:, 8:16], bb[:, 0:8], AF.Exp, bias=rr[:, 1:2], scale=1.0,
                        accum_out=rr[:, 2:3])))(),
                    waits=[("ve", vneg)], inc="ex")
        AV((lambda t0=tt: (lambda eng: eng.reciprocal(chp[:, t0:t0 + 1], rr[:, 2:3])))(),
           waits=[("ex", sel)])
        AV((lambda bb=bank: (lambda eng: eng.tensor_single_scalar(
            rr[:, 16:24], bb[:, 0:8], rr[:, 0:1], ALU.is_equal)))())
        AV(lambda eng: eng.tensor_scalar_add(rr[:, 24:25], rr[:, 16:17], 0.0))
        AV(lambda eng: eng.tensor_tensor(rr[:, 25:32], rr[:, 17:24], rr[:, 16:23], ALU.add))
        AV(lambda eng: eng.tensor_scalar_add(rr[:, 32:34], rr[:, 24:26], 0.0))
        AV(lambda eng: eng.tensor_tensor(rr[:, 34:40], rr[:, 26:32], rr[:, 24:30], ALU.add))
        AV(lambda eng: eng.tensor_scalar_add(rr[:, 40:44], rr[:, 32:36], 0.0))
        AV(lambda eng: eng.tensor_tensor(rr[:, 44:48], rr[:, 36:40], rr[:, 32:36], ALU.add))
        AV(lambda eng: eng.tensor_single_scalar(rr[:, 48:56], rr[:, 40:48], 1.0, ALU.is_equal))
        vfoh = AV((lambda t0=tt: (lambda eng: eng.tensor_tensor(
            foh_all[:, 8 * t0:8 * (t0 + 1)], rr[:, 16:24], rr[:, 48:56], ALU.mult)))())
        sfb = P.add("scalar",
                    (lambda t0=tt: (lambda eng: eng.copy(
                        fohbf[:, 8 * t0:8 * (t0 + 1)], foh_all[:, 8 * t0:8 * (t0 + 1)])))(),
                    waits=[("ve", vfoh)], inc="ex")

        def mk_cs(t0=tt):
            def fn(eng):
                eng.matmul(pf[2 + (t0 % 2)][:, 0:8], ltssb[:, :], fohbf[:, 8 * t0:8 * (t0 + 1)],
                           start=True, stop=True)
                return eng.matmul(pf[2 + (t0 % 2)][0:1, 8:16], onescb[:, :],
                                  fohbf[:, 8 * t0:8 * (t0 + 1)], start=False, stop=True,
                                  skip_group_check=True)
            return fn
        csw = [("ex", sfb), ("csth", c_cst[0]), ("cstg", c_cst[1])]
        if tt >= 2:
            csw.append(("ve", v_runoff[tt - 2]))
        csm = P.add("tensor", mk_cs(), waits=csw, inc="mm")
        rbw = [("ve", v_runoff[tt - 1] if tt > 0 else v_rz)]
        rbm = P.add("tensor",
                    (lambda t0=tt: (lambda eng: eng.matmul(
                        pf[2 + (t0 % 2)][:, 16:24], onesrf[:, :],
                        runoff[0:1, E * (t0 % 2):E * (t0 % 2) + E], start=False, stop=True,
                        skip_group_check=True)))(),
                    waits=rbw, inc="mm")
        sit = P.add("scalar",
                    (lambda t0=tt: (lambda eng: eng.copy(rr[:, 24:32],
                                                         pf[2 + (t0 % 2)][:, 0:8])))(),
                    waits=[("mm", csm)], inc="ex")
        AV((lambda t0=tt: (lambda eng: eng.tensor_tensor(
            rr[:, 32:40], rr[:, 24:32], pf[2 + (t0 % 2)][:, 16:24], ALU.add)))(),
           waits=[("ex", sit), ("mm", rbm)])
        AV((lambda t0=tt: (lambda eng: eng.tensor_tensor(
            rr[:, 40:48], rr[:, 32:40], foh_all[:, 8 * t0:8 * (t0 + 1)], ALU.mult)))())
        AV((lambda t0=tt: (lambda eng: eng.reduce_sum(posch[:, t0:t0 + 1], rr[:, 40:48],
                                                      AX.X)))())
        AV((lambda t0=tt: (lambda eng: eng.tensor_tensor(
            rr[:, 48:56], foh_all[:, 8 * t0:8 * (t0 + 1)], basesb[:, :], ALU.mult)))())
        AV(lambda eng: eng.reduce_sum(rr[:, 3:4], rr[:, 48:56], AX.X))
        AV((lambda t0=tt: (lambda eng: eng.tensor_tensor(rr[:, 4:5], posch[:, t0:t0 + 1],
                                                         rr[:, 3:4], ALU.add)))())
        AV(lambda eng: eng.tensor_scalar_min(rr[:, 5:6], rr[:, 4:5], float(AGROWS - 1)))
        vsi = AV((lambda t0=tt: (lambda eng: eng.tensor_copy(sloti[:, t0:t0 + 1],
                                                             rr[:, 5:6])))())
        v_runoff[tt] = AV((lambda t0=tt: (lambda eng: eng.tensor_tensor(
            runoff[0:1, E * ((t0 + 1) % 2):E * ((t0 + 1) % 2) + E],
            runoff[0:1, E * (t0 % 2):E * (t0 % 2) + E],
            pf[2 + (t0 % 2)][0:1, 8:16], ALU.add)))(),
            waits=[("mm", csm)])
        v_last[tt] = v_runoff[tt]
        if tt == 0:
            vsi_all = {}
        vsi_all[tt] = vsi

    cpar = NTT % 2
    g1 = P.add("gpsimd", _dma(d_ccin.ap(), runoff[0:1, E * cpar:E * cpar + E]),
               waits=[("ve", v_runoff[NTT - 1])], inc="ccd", n=16)
    g2 = P.add("gpsimd",
               lambda eng: eng.collective_compute(
                   "AllGather", ALU.bypass, replica_groups=[list(range(8))],
                   ins=[d_ccin.ap().opt()], outs=[d_ccout.ap().opt()]),
               waits=[("ccd", 16)], inc="cc")
    g3 = P.add("gpsimd", _dma(ccsb[:, :], d_ccout[:, :]), waits=[("cc", g2)], inc="ccd", n=16)
    com = P.add("tensor", lambda eng: eng.matmul(pf[3][0:1, 24:32], cmsb[:, :], ccsb[:, :],
                                                 start=False, stop=True, skip_group_check=True),
                waits=[("ccd", 32), ("csth", c_cst[0]), ("cstg", c_cst[1]), ("ve", v_runoff[NTT - 1])], inc="mm")
    cev = P.add("scalar", lambda eng: eng.copy(rr[0:1, 56:64], pf[3][0:1, 24:32]),
                waits=[("mm", com)], inc="ex")
    cob = P.add("tensor", lambda eng: eng.matmul(pf[3][:, 32:40], onesrf[:, :], rr[0:1, 56:64],
                                                 start=False, stop=True, skip_group_check=True),
                waits=[("ex", cev)], inc="mm")
    cev2 = P.add("scalar", lambda eng: eng.copy(cofT[:, :], pf[3][:, 32:40]),
                 waits=[("mm", cob)], inc="ex")

    at_ev = {}
    sc_ind = {}
    for tt in range(NTT):
        for dvc in range(8):
            if dvc >= 2:
                tw = [("ex", at_ev[(tt, dvc - 2)])]
            elif tt > 0:
                tw = [("ex", at_ev[(tt - 1, 6 + dvc)])]
            else:
                tw = [("ve", c_attn_ve), ("csth", c_cst[0]), ("cstg", c_cst[1])]
            tmm = P.add("tensor",
                        (lambda t0=tt, dd=dvc: (lambda eng: eng.matmul(
                            pbf[dd % 2][:, :],
                            A_fm[:, 2048 * dd + 128 * t0:2048 * dd + 128 * (t0 + 1)],
                            identsb[:, :], start=True, stop=True)))(),
                        waits=tw, inc="mm")
            at_ev[(tt, dvc)] = P.add("scalar",
                                     (lambda t0=tt, dd=dvc: (lambda eng: eng.copy(
                                         A_tok[:, 1024 * t0 + 128 * dd:
                                               1024 * t0 + 128 * (dd + 1)],
                                         pbf[dd % 2][:, :])))(),
                                     waits=[("mm", tmm)], inc="ex")
        sc_ind[tt] = P.add("gpsimd",
                           (lambda t0=tt: (lambda eng: eng.indirect_dma_start(
                               d_agscr.ap(), IndirectOffsetOnAxis(sloti[:, t0:t0 + 1], 0),
                               A_tok[:, 1024 * t0:1024 * (t0 + 1)], None)))(),
                           waits=[("ex", at_ev[(tt, 7)]), ("ve", vsi_all[tt]),
                                  ("zf", c_zfill)], inc="ind", n=16)
    c_scat = sc_ind[NTT - 1]
    dps.close()

    fps = ExitStack()
    yps = [fps.enter_context(nc.psum_tensor(f"yps{i}", [128, 512], F32)) for i in range(7)]

    tr_tmp = {}
    ag_done = {}
    h_done = {}
    hq_done = {}
    yq_done = {}
    yev_done = {}
    c_h_ex = {}
    ysb_guard = {0: 0, 1: 0}

    def plan_ag(e, waits_extra):
        epar = e % 2
        trex = None
        for j in range(4):
            rows = SCW[j]
            lw = [("ind", c_scat)] + list(waits_extra)
            if j > 0:
                lw.append(("ex", trex))
            lc = P.add("gpsimd",
                       (lambda ee=e, jj=j, rw=rows: (lambda eng: eng.dma_start(
                           out=agld[0:rw, :],
                           in_=d_agscr[CL * ee + 128 * jj:CL * ee + 128 * jj + rw, :])))(),
                       waits=lw, inc="agld", n=16)
            for ci in range(8):
                idx = j * 8 + ci
                tb = yps[4 + (idx % 2)]
                trw = [("agld", lc), ("csth", c_cst[0]), ("cstg", c_cst[1])] + list(waits_extra)
                if idx >= 2:
                    trw.append(("ex", tr_tmp[idx - 2]))
                elif (idx + 30) in tr_tmp:
                    trw.append(("ex", tr_tmp[idx + 30]))
                tmm = P.add("tensor",
                            (lambda rw=rows, cc=ci, bb=tb: (lambda eng: eng.matmul(
                                bb[:, 0:rw], agld[0:rw, 128 * cc:128 * (cc + 1)],
                                identsb[0:rw, 0:rw], start=True, stop=True)))(),
                            waits=trw, inc="mm")
                tr_tmp[idx] = P.add("scalar",
                                    (lambda jj=j, cc=ci, rw=rows, pp=epar, bb=tb:
                                     (lambda eng: eng.copy(
                                         agbuf[pp][:, CL * cc + 128 * jj:
                                                   CL * cc + 128 * jj + rw],
                                         bb[:, 0:rw])))(),
                                    waits=[("mm", tmm)], inc="ex")
                trex = tr_tmp[idx]
        return trex

    FSUB = int(os.environ.get("FSUB", "9"))
    ELIM = int(os.environ.get("ELIM", str(E)))
    ag_done[0] = plan_ag(0, [("ex", cev2), ("ve", v_runoff[NTT - 1])])

    for e in range(ELIM):
        epar = e % 2
        if FSUB < 2:
            continue
        def w1_load(ee, ff, waits):
            return P.add("gpsimd",
                         (lambda: (lambda eng: eng.dma_start(
                             out=w1q[ff % 2][:, :].rearrange("p (i f) -> p i f", i=8),
                             in_=d_w1[ee, :, 512 * ff:512 * (ff + 1)]
                             .rearrange("(i p) f -> p i f", p=128))))(),
                         waits=waits, inc="w1a" if ff % 2 == 0 else "w1b", n=16)
        w1c = {}
        for f8 in (0, 1):
            wv = [("csth", c_cst[0]), ("cstg", c_cst[1])]
            prev = (e - 1, f8 + 6)
            if prev in hq_done:
                wv.append(("mm", hq_done[prev]))
            w1c[f8] = w1_load(e, f8, wv)

        if FSUB < 3:
            continue
        hev = {}
        for f8 in range(8):
            for fc4 in range(4):
                fg = 4 * f8 + fc4
                bank = yps[fg % 2]

                def mk_h(ff=f8, cc=fc4, bb=bank, pp=epar):
                    def fn(eng):
                        last = None
                        for i in range(8):
                            last = eng.matmul(
                                bb[:, 0:CL],
                                w1q[ff % 2][:, 512 * i + 128 * cc:512 * i + 128 * (cc + 1)],
                                agbuf[pp][:, CL * i:CL * (i + 1)],
                                start=(i == 0), stop=(i == 7))
                        return last
                    return fn
                hw = [("w1a" if f8 % 2 == 0 else "w1b", w1c[f8]), ("ex", ag_done[e])]
                if fg >= 2:
                    hw.append(("ex", hev[fg - 2]))
                elif e > 0:
                    hw.append(("ex", yev_done[(e - 1, 1)]))
                hm = P.add("tensor", mk_h(), waits=hw, inc="mm")
                gw2 = [("mm", hm)]
                if e == 0 and fg == 0:
                    gw2.append(("csth", c_cst[0]))
                    gw2.append(("cstg", c_cst[1]))
                if fg == 0 and (e - 1, 3) in yq_done:
                    gw2.append(("mm", yq_done[(e - 1, 3)]))
                hev[fg] = P.add("scalar",
                                (lambda ee=e, gg=fg, bb=bank: (lambda eng: eng.activation(
                                    hsb[:, CL * gg:CL * (gg + 1)], bb[:, 0:CL], AF.Gelu,
                                    bias=b1all[:, 32 * ee + gg:32 * ee + gg + 1],
                                    scale=1.0)))(),
                                waits=gw2, inc="ex")
            hq_done[(e, f8)] = hm
            if f8 + 2 < 8:
                w1c[f8 + 2] = w1_load(e, f8 + 2, [("mm", hm)])
        h_done[e] = hm
        c_h_ex[e] = hev[31]

        if e + 1 < ELIM and FSUB >= 4:
            wx = [("ex", yev_done[(e - 1, 1)])] if (e - 1, 1) in yev_done else []
            ag_done[e + 1] = plan_ag(e + 1, wx)

        if FSUB < 5:
            continue

        def w2_load(ee, ff, waits):
            return P.add("gpsimd",
                         (lambda: (lambda eng: eng.dma_start(
                             out=w2q[ff % 2][:, :].rearrange("p (g f) -> p g f", g=8),
                             in_=d_w2[ee, 1024 * ff:1024 * (ff + 1), :]
                             .rearrange("(g p) f -> p g f", p=128))))(),
                         waits=waits, inc="w2a" if ff % 2 == 0 else "w2b", n=16)
        w2c = {}
        for fq in (0, 1):
            wv = []
            prev = (e - 1, fq + 2)
            if prev in yq_done:
                wv.append(("mm", yq_done[prev]))
            w2c[fq] = w2_load(e, fq, wv)

        if FSUB < 6:
            continue
        for fq in range(4):
            for g8 in range(8):
                fg = 8 * fq + g8

                def mk_yq(ff=fq, gg=g8, fgg=fg):
                    def fn(eng):
                        last = None
                        for sc in range(4):
                            rw = SCW[sc]
                            for dvh in range(2):
                                if sc < 3:
                                    ob = yps[2 * sc + dvh][:, 0:512]
                                    st = (fgg == 0)
                                else:
                                    ob = yps[6][64 * dvh:64 * dvh + 64, 0:512]
                                    st = (fgg == 0)
                                last = eng.matmul(
                                    ob,
                                    hsb[:, CL * fgg + SCO[sc]:CL * fgg + SCO[sc] + rw],
                                    w2q[ff % 2][:, 1024 * gg + 512 * dvh:
                                                1024 * gg + 512 * (dvh + 1)],
                                    start=st, stop=False, skip_group_check=True)
                        return last
                    return fn
                yw = [("w2a" if fq % 2 == 0 else "w2b", w2c[fq]), ("ex", hev[fg])]
                if fg == 0:
                    yw.append(("ex", c_h_ex[e]))
                    if (e + 1) in ag_done:
                        yw.append(("ex", ag_done[e + 1]))
                ym = P.add("tensor", mk_yq(), waits=yw, inc="mm")
            yq_done[(e, fq)] = ym
            if fq + 2 < 4:
                w2c[fq + 2] = w2_load(e, fq + 2, [("mm", ym)])

        if FSUB < 7:
            continue
        for dvh in range(2):
            for sc in range(4):
                rw = SCW[sc]
                if sc < 3:
                    ob = yps[2 * sc + dvh]
                    rs = slice(0, rw)
                else:
                    ob = yps[6]
                    rs = slice(64 * dvh, 64 * dvh + 64)
                par = (4 * dvh + sc) % 2

                def mk_b(ee=e, dh=dvh, bb=ob, r=rs, rw=rw):
                    def fn(eng):
                        return eng.matmul(bb[r, 0:512], onesrb[0:1, 0:rw],
                                          b2all[0:1, D * ee + 512 * dh:D * ee + 512 * (dh + 1)],
                                          start=False, stop=True, skip_group_check=True)
                    return fn
                bw = [("mm", yq_done[(e, 3)]), ("csth", c_cst[0]), ("cstg", c_cst[1])]
                if sc == 3 and dvh == 1:
                    bw.append(("ex", yev_done[(e, 0)]))
                bm = P.add("tensor", mk_b(), waits=bw, inc="mm")
                svw = [("mm", bm)]
                if ysb_guard[par] > 0:
                    svw.append(("ysb0" if par == 0 else "ysb1", ysb_guard[par]))
                sv = P.add("scalar",
                           (lambda bb=ob, r=rs, rw=rw, pp=par:
                            (lambda eng: eng.copy(ysb[pp][0:rw, :], bb[r, 0:512])))(),
                           waits=svw, inc="ex")
                yst = P.add("scalar",
                            (lambda ee=e, dh=dvh, ss=sc, rw=rw, pp=par:
                             (lambda eng: eng.dma_start(
                                 out=d_yscr[CL * ee + SCO[ss]:CL * ee + SCO[ss] + rw,
                                            512 * dh:512 * (dh + 1)],
                                 in_=ysb[pp][0:rw, :])))(),
                            waits=[("ex", sv)], inc="ysb0" if par == 0 else "ysb1", n=16)
                ysb_guard[par] = yst
                yev_done[(e, dvh)] = sv
    c_yall = (C["ysb0"], C["ysb1"])

    prev_z = None
    prev_sq = None
    for tt in range(NTT):
        gw = [("ysb0", c_yall[0]), ("ysb1", c_yall[1]), ("zf", c_zfill),
              ("ve", vsi_all[tt])]
        if prev_z is not None:
            gw.append(("ve", prev_z))
        gth = P.add("gpsimd",
                    (lambda t0=tt: (lambda eng: eng.indirect_dma_start(
                        ytok[:, :], None, d_yscr.ap(),
                        IndirectOffsetOnAxis(sloti[:, t0:t0 + 1], 0))))(),
                    waits=gw, inc="ind", n=16)

        def AV(fn, waits=(), inc="ve"):
            return P.add("vector", fn, waits=waits, inc=inc)

        AV((lambda t0=tt: (lambda eng: eng.tensor_tensor(
            rr[:, 8:16], foh_all[:, 8 * t0:8 * (t0 + 1)], cofT[:, :], ALU.mult)))(),
           waits=[("ex", cev2)])
        AV(lambda eng: eng.reduce_sum(rr[:, 0:1], rr[:, 8:16], AX.X))
        AV((lambda t0=tt: (lambda eng: eng.tensor_tensor(rr[:, 1:2], posch[:, t0:t0 + 1],
                                                         rr[:, 0:1], ALU.add)))())
        AV(lambda eng: eng.tensor_single_scalar(rr[:, 2:3], rr[:, 1:2], CCAP, ALU.is_lt))
        AV((lambda t0=tt: (lambda eng: eng.tensor_tensor(rr[:, 3:4], chp[:, t0:t0 + 1],
                                                         rr[:, 2:3], ALU.mult)))())
        zw = [("ind", gth)]
        if prev_sq is not None:
            zw.append(("ex", prev_sq))
        vz2 = AV((lambda t0=tt: (lambda eng: eng.scalar_tensor_tensor(
            z2b[:, :], ytok[:, :], rr[:, 3:4],
            A_tok[:, 1024 * t0:1024 * (t0 + 1)], ALU.mult, ALU.add)))(), waits=zw)
        prev_z = vz2
        vs1 = AV(lambda eng: eng.reduce_sum(rr[:, 4:5], z2b[:, :], AX.X))
        ssq = P.add("scalar", lambda eng: eng.activation(sqb[:, :], z2b[:, :], AF.Square),
                    waits=[("ve", vz2)], inc="ex")
        prev_sq = ssq
        vs2 = AV(lambda eng: eng.reduce_sum(rr[:, 5:6], sqb[:, :], AX.X),
                 waits=[("ex", ssq)])
        AV(lambda eng: eng.tensor_scalar_mul(rr[:, 6:7], rr[:, 4:5], 1.0 / D),
           waits=[("ve", vs1)])
        AV(lambda eng: eng.tensor_scalar_mul(rr[:, 7:8], rr[:, 5:6], 1.0 / D))
        AV(lambda eng: eng.tensor_tensor(rr[:, 17:18], rr[:, 6:7], rr[:, 6:7], ALU.mult))
        AV(lambda eng: eng.tensor_tensor(rr[:, 18:19], rr[:, 7:8], rr[:, 17:18], ALU.subtract))
        vve = AV(lambda eng: eng.tensor_scalar_add(rr[:, 19:20], rr[:, 18:19], EPS))
        sl2 = P.add("scalar", lambda eng: eng.activation(rr[:, 20:21], rr[:, 19:20], AF.Ln),
                    waits=[("ve", vve)], inc="ex")
        sr2 = P.add("scalar", lambda eng: eng.activation(rr[:, 21:22], rr[:, 20:21], AF.Exp,
                                                         scale=-0.5),
                    waits=[("ex", sl2)], inc="ex")
        AV(lambda eng: eng.tensor_tensor(rr[:, 22:23], rr[:, 6:7], rr[:, 21:22], ALU.mult),
           waits=[("ex", sr2)])
        vnm = AV(lambda eng: eng.tensor_scalar_mul(rr[:, 23:24], rr[:, 22:23], -1.0))
        sow = [("ve", vnm)]
        if tt >= 2:
            sow.append(("out0" if tt % 2 == 0 else "out1", 16 * (tt // 2)))
        sout = P.add("scalar",
                     (lambda t0=tt: (lambda eng: eng.activation(
                         outb[t0 % 2][:, :], z2b[:, :], AF.Identity,
                         bias=rr[:, 23:24], scale=rr[:, 21:22])))(),
                     waits=sow, inc="ex")
        prev_sq = sout
        P.add("scalar",
              (lambda t0=tt: (lambda eng: eng.dma_start(
                  out=d_out[128 * t0:128 * (t0 + 1), :], in_=outb[t0 % 2][:, :])))(),
              waits=[("ex", sout)], inc="out0" if tt % 2 == 0 else "out1", n=16)
    P.add("scalar", None, waits=[("out0", 16 * (NTT // 2)), ("out1", 16 * (NTT // 2))])
    P.add("sync", None, waits=[("out0", 16 * (NTT // 2)), ("out1", 16 * (NTT // 2))])

    fps.close()
    ffn.close()

    sems = {}
    semstack = ExitStack()
    for name in P.cnt:
        sems[name] = semstack.enter_context(nc.semaphore(f"s_{name}"))

    barctr = {}
    with nc.Block() as block:
        def emitter(ops):
            def run(eng):
                hw = {}
                own = "ve" if eng is nc.vector else ("ex" if eng is nc.scalar else None)
                nown = 0
                for fn, waits, inc, n in ops:
                    if own is not None and nown > 0 and hw.get(own, 0) < nown:
                        eng.wait_ge(sems[own], nown)
                        hw[own] = nown
                    for s, v in waits:
                        if hw.get(s, 0) < v:
                            eng.wait_ge(sems[s], v)
                            hw[s] = v
                    if own is not None and inc == own:
                        nown += n
                    if fn is None:
                        continue
                    if fn == "BAR_INC":
                        k = barctr.setdefault(id(eng), 0)
                        barctr[id(eng)] = k + 1
                        if eng is nc.scalar:
                            eng.memzero(dumb[0:1, k:k + 1]).then_inc(sems["bar"], 1)
                        elif eng is nc.vector:
                            eng.memset(dumb[0:1, 32 + k:33 + k], 0.0
                                       ).then_inc(sems["bar"], 1)
                        elif eng is nc.tensor:
                            eng.nop().then_inc(sems["bar"], 1)
                        else:
                            eng.sem_inc(sems["bar"], 1)
                        continue
                    ins = fn(eng)
                    if inc is not None:
                        ins.then_inc(sems[inc], n)
            return run

        block.tensor(emitter(P.ops["tensor"]))
        block.vector(emitter(P.ops["vector"]))
        block.scalar(emitter(P.ops["scalar"]))
        block.gpsimd(emitter(P.ops["gpsimd"]))
        block.sync(emitter(P.ops["sync"]))

    semstack.close()
    return nc


_NC_CACHE = []


def kernel(**inputs):
    f32 = np.float32
    x = np.asarray(inputs["L0_attn_x"], dtype=f32)
    w_q = np.asarray(inputs["L0_attn_w_q"], dtype=f32)
    b_q = np.asarray(inputs["L0_attn_b_q"], dtype=f32)
    w_k = np.asarray(inputs["L0_attn_w_k"], dtype=f32)
    b_k = np.asarray(inputs["L0_attn_b_k"], dtype=f32)
    w_v = np.asarray(inputs["L0_attn_w_v"], dtype=f32)
    b_v = np.asarray(inputs["L0_attn_b_v"], dtype=f32)
    k_0 = np.asarray(inputs["L0_attn_k_0"], dtype=f32)
    v_0 = np.asarray(inputs["L0_attn_v_0"], dtype=f32)
    rw = np.asarray(inputs["L0_ffn_router_w"], dtype=f32)
    w1 = np.asarray(inputs["L0_ffn_w1"], dtype=f32)
    b1 = np.asarray(inputs["L0_ffn_b1"], dtype=f32)
    w2 = np.asarray(inputs["L0_ffn_w2"], dtype=f32)
    b2 = np.asarray(inputs["L0_ffn_b2"], dtype=f32)

    cm = np.ascontiguousarray
    shared = {
        "wqT": cm(w_q[0:128].T), "wkT": cm(w_k[0:128].T),
        "wqtl": cm(w_q[128].reshape(8, 128).T), "wktl": cm(w_k[128].reshape(8, 128).T),
        "bq": cm(b_q[0:128].reshape(128, 1)), "bk": cm(b_k[0:128].reshape(128, 1)),
        "btl": cm(np.stack([b_q[128:129], b_k[128:129]], axis=1)),
        "k0c": cm(k_0[0:128].reshape(128, 1)), "k0tl": cm(k_0[128].reshape(1, 1)),
        "v0r": cm(v_0.reshape(1, D)), "bvr": cm(b_v.reshape(1, D)),
        "wvT": cm(w_v.T), "rT": cm(rw.T),
        "w1": w1, "w2": w2, "b2": b2,
        "b1r": cm(b1.reshape(E, 32, 128).transpose(0, 2, 1)),
    }
    in_maps = []
    for c in range(8):
        b = c // 4
        r0 = (c % 4) * T
        m = dict(shared)
        m["x"] = cm(x[b])
        m["xT"] = cm(x[b].T)
        m["xqT"] = cm(x[b, r0:r0 + T].T)
        m["cmask"] = cm((np.arange(8) < c).astype(f32).reshape(8, 1))
        in_maps.append(m)

    if not _NC_CACHE:
        _NC_CACHE.append(build_program())
    nc = _NC_CACHE[0]
    res = run_bass_kernel_spmd(nc, in_maps, core_ids=list(range(8)))
    out = np.empty((2, S, D), dtype=f32)
    for c in range(8):
        out[c // 4, (c % 4) * T:(c % 4) * T + T] = res.results[c]["out"]
    return out


if __name__ == "__main__":
    build_program()
    print("build ok")
